# version 02
# baseline (speedup 1.0000x reference)
import numpy as np
import ml_dtypes
from contextlib import ExitStack

import concourse.bass as bass
import concourse.mybir as mybir
from concourse.bass_utils import run_bass_kernel_spmd

F32 = mybir.dt.float32
BF16 = mybir.dt.bfloat16
AF = mybir.ActivationFunctionType
ALU = mybir.AluOpType
AX = mybir.AxisListType

B = 4
S = 4096
D = 2048
NH = 16
DH = 128
H = 5632
OWN = 2048
DT = D // 128
HT = H // 128
NQT = S // 128
EPS_LN = 1e-5
EPS_Q = 1e-8
W_ELEMS = H * D
INVSQ = float(1.0 / np.sqrt(DH))


class Prog:
    def __init__(self, nc, es):
        self.nc = nc
        self.sems = {}
        for name in ("qS", "qG", "sT", "sV", "sS"):
            self.sems[name] = [es.enter_context(nc.semaphore(name)), 0]

    def bump(self, name, n):
        s = self.sems[name]
        s[1] += n
        return s[0]

    def val(self, name):
        return self.sems[name][1]

    def h(self, name):
        return self.sems[name][0]


def _finish(p, block):
    @block.sync
    def _(sync):
        sync.wait_ge(p.h("qS"), p.val("qS"))
        sync.wait_ge(p.h("qG"), p.val("qG"))


def build_program(taps=(), stage=99.0):
    nc = bass.Bass("TRN2", target_bir_lowering=False)

    xb = nc.dram_tensor("xb", [S, D], F32, kind="ExternalInput").ap()
    wqkv = nc.dram_tensor("wqkv", [D, 3 * D], F32, kind="ExternalInput").ap()
    wo = nc.dram_tensor("wo", [D, D], F32, kind="ExternalInput").ap()
    w1 = nc.dram_tensor("w1", [H, D], F32, kind="ExternalInput").ap()
    w2 = nc.dram_tensor("w2", [H, D], F32, kind="ExternalInput").ap()
    w3 = nc.dram_tensor("w3", [D, H], F32, kind="ExternalInput").ap()
    out = nc.dram_tensor("out", [OWN, D], F32, kind="ExternalOutput").ap()

    tap_aps = {}

    def tap(name, shape, dtype=F32):
        if name in taps:
            tap_aps[name] = nc.dram_tensor("tap_" + name, shape, dtype,
                                           kind="ExternalOutput").ap()
            return tap_aps[name]
        return None

    xn_dram = nc.dram_tensor("xn_dram", [S, D], BF16, kind="Internal").ap()
    qT_dram = nc.dram_tensor("qT_dram", [D, S], BF16, kind="Internal").ap()
    kT_dram = nc.dram_tensor("kT_dram", [D, S], BF16, kind="Internal").ap()
    vT_dram = nc.dram_tensor("vT_dram", [D, OWN], BF16, kind="Internal").ap()
    w1q_dram = nc.dram_tensor("w1q_dram", [H, D], BF16, kind="Internal").ap()
    w2q_dram = nc.dram_tensor("w2q_dram", [H, D], BF16, kind="Internal").ap()
    w3q_dram = nc.dram_tensor("w3q_dram", [D, H], BF16, kind="Internal").ap()
    w1qT_dram = nc.dram_tensor("w1qT_dram", [D, H], BF16, kind="Internal").ap()
    w2qT_dram = nc.dram_tensor("w2qT_dram", [D, H], BF16, kind="Internal").ap()
    w3qT_dram = nc.dram_tensor("w3qT_dram", [H, D], BF16, kind="Internal").ap()
    attnT_dram = nc.dram_tensor("attnT_dram", [D, OWN], BF16, kind="Internal").ap()

    ones_col_f32 = nc.inline_tensor(np.ones((128, 1), dtype=np.float32), name="c_ones_col")
    ones_sq_f32 = nc.inline_tensor(np.ones((128, 128), dtype=np.float32), name="c_ones_sq")

    es = ExitStack()
    with es:
        block = es.enter_context(nc.Block())
        p = Prog(nc, es)

        ones_c = es.enter_context(nc.sbuf_tensor("ones_c", [128, 1], F32))
        ones_sq = es.enter_context(nc.sbuf_tensor("ones_sq", [128, 128], F32))
        h_sb = es.enter_context(nc.sbuf_tensor("h_sb", [128, 4], F32))
        nh_sb = es.enter_context(nc.sbuf_tensor("nh_sb", [128, 4], F32))

        @block.sync
        def _(sync):
            sync.dma_start(out=ones_c[:], in_=ones_col_f32.ap()).then_inc(p.bump("qS", 16), 16)
            sync.dma_start(out=ones_sq[:], in_=ones_sq_f32.ap()).then_inc(p.bump("qS", 16), 16)

        ws = [(w1, w1q_dram), (w2, w2q_dram), (w3, w3q_dram)]
        w_tiles = []
        for mi, (wa, _) in enumerate(ws):
            for r0 in range(0, wa.shape[0], 128):
                w_tiles.append((mi, r0))
        NT0 = len(w_tiles)

        p0es = ExitStack()
        with p0es:
            wbuf = [p0es.enter_context(nc.sbuf_tensor(f"p0_w{i}", [128, H], F32)) for i in range(2)]
            asum = p0es.enter_context(nc.sbuf_tensor("asum", [128, NT0], F32))
            atot = p0es.enter_context(nc.sbuf_tensor("atot", [128, 4], F32))
            qtile = [p0es.enter_context(nc.sbuf_tensor(f"p0_q{i}", [128, H], BF16)) for i in range(2)]
            t1 = p0es.enter_context(nc.sbuf_tensor("p0_t1", [128, H], BF16))
            t2 = p0es.enter_context(nc.sbuf_tensor("p0_t2", [128, H], BF16))
            trbuf = [p0es.enter_context(nc.sbuf_tensor(f"p0_tr{i}", [128, H], BF16)) for i in range(2)]
            psum_h = p0es.enter_context(nc.psum_tensor("psum_h", [128, 4], F32))

            qS0 = p.val("qS")
            sV0 = p.val("sV")

            @block.sync
            def _(sync):
                for i, (mi, r0) in enumerate(w_tiles):
                    wa = ws[mi][0]
                    if i >= 2:
                        sync.wait_ge(p.h("sV"), sV0 + i - 1)
                    sync.dma_start(out=wbuf[i % 2][:, 0:wa.shape[1]],
                                   in_=wa[r0:r0 + 128, :]).then_inc(p.bump("qS", 16), 16)

            @block.vector
            def _(vector):
                for i, (mi, r0) in enumerate(w_tiles):
                    ncols = ws[mi][0].shape[1]
                    vector.wait_ge(p.h("qS"), qS0 + 16 * (i + 1))
                    vector.tensor_reduce(asum[:, i:i + 1], wbuf[i % 2][:, 0:ncols], axis=AX.X,
                                         op=ALU.add, apply_absolute_value=True).then_inc(p.bump("sV", 1), 1)
                vector.wait_ge(p.h("sV"), p.val("sV"))
                n1 = 0
                for mi in range(3):
                    cnt = sum(1 for (m, _) in w_tiles if m == mi)
                    ins = vector.tensor_reduce(atot[:, mi:mi + 1], asum[:, n1:n1 + cnt],
                                               axis=AX.X, op=ALU.add)
                    if mi == 2:
                        ins.then_inc(p.bump("sV", 1), 1)
                    n1 += cnt
                vector.wait_ge(p.h("sV"), p.val("sV"))

            @block.tensor
            def _(tensor):
                tensor.wait_ge(p.h("sV"), p.val("sV"))
                tensor.wait_ge(p.h("qS"), 32)
                for mi in range(3):
                    ins = tensor.matmul(psum_h[:, mi:mi + 1], ones_sq[:], atot[:, mi:mi + 1],
                                  start=True, stop=True)
                ins.then_inc(p.bump("sT", 1), 1)

            @block.vector
            def _(vector):
                vector.wait_ge(p.h("sT"), p.val("sT"))
                vector.tensor_scalar(h_sb[:, 0:3], psum_h[:, 0:3], 1.0 / W_ELEMS, None,
                                     op0=ALU.mult).then_inc(p.bump("sV", 1), 1)
                vector.wait_ge(p.h("sV"), p.val("sV"))
                vector.tensor_scalar(h_sb[:, 0:3], h_sb[:, 0:3], EPS_Q, 0.5,
                                     op0=ALU.add, op1=ALU.mult).then_inc(p.bump("sV", 1), 1)
                vector.wait_ge(p.h("sV"), p.val("sV"))
                vector.tensor_scalar(nh_sb[:, 0:3], h_sb[:, 0:3], -1.0, None,
                                     op0=ALU.mult).then_inc(p.bump("sV", 1), 1)

            t_at = tap("atot", [128, 4])
            if t_at is not None:
                @block.sync
                def _(sync):
                    sync.wait_ge(p.h("sV"), p.val("sV"))
                    sync.dma_start(out=t_at, in_=atot[:]).then_inc(p.bump("qS", 16), 16)
            t_as = tap("asum", [128, NT0])
            if t_as is not None:
                @block.sync
                def _(sync):
                    sync.wait_ge(p.h("sV"), p.val("sV"))
                    sync.dma_start(out=t_as, in_=asum[:]).then_inc(p.bump("qS", 16), 16)

            t_gamma = tap("hthr", [1, 4])
            if t_gamma is not None:
                @block.sync
                def _(sync):
                    sync.wait_ge(p.h("sV"), p.val("sV"))
                    sync.dma_start(out=t_gamma, in_=h_sb[0:1, :]).then_inc(p.bump("qS", 16), 16)

            if stage < 0.2:
                _finish(p, block)
                return nc, tap_aps

            sVq = p.val("sV")
            qSq = p.val("qS")
            qGq = p.val("qG")

            @block.sync
            def _(sync):
                sync.wait_ge(p.h("sV"), p.val("sV"))
                for i, (mi, r0) in enumerate(w_tiles):
                    wa = ws[mi][0]
                    if i >= 2:
                        sync.wait_ge(p.h("sV"), sVq + i - 1)
                    sync.dma_start(out=wbuf[i % 2][:, 0:wa.shape[1]],
                                   in_=wa[r0:r0 + 128, :]).then_inc(p.bump("qS", 16), 16)

            @block.vector
            def _(vector):
                for i, (mi, r0) in enumerate(w_tiles):
                    ncols = ws[mi][0].shape[1]
                    vector.wait_ge(p.h("qS"), qSq + 16 * (i + 1))
                    if i >= 2:
                        vector.wait_ge(p.h("qG"), qGq + 16 * (i - 1))
                    vector.tensor_scalar(t1[:, 0:ncols], wbuf[i % 2][:, 0:ncols],
                                         h_sb[:, mi:mi + 1], None, op0=ALU.is_gt)
                    vector.tensor_scalar(t2[:, 0:ncols], wbuf[i % 2][:, 0:ncols],
                                         nh_sb[:, mi:mi + 1], None, op0=ALU.is_lt)
                    vector.tensor_tensor(qtile[i % 2][:, 0:ncols], t1[:, 0:ncols], t2[:, 0:ncols],
                                         op=ALU.subtract).then_inc(p.bump("sV", 1), 1)

            @block.gpsimd
            def _(gpsimd):
                for i, (mi, r0) in enumerate(w_tiles):
                    qd = ws[mi][1]
                    gpsimd.wait_ge(p.h("sV"), sVq + i + 1)
                    gpsimd.dma_start(out=qd[r0:r0 + 128, :],
                                     in_=qtile[i % 2][:, 0:qd.shape[1]]).then_inc(p.bump("qG", 16), 16)

            t_q = tap("w1q", [H, D], BF16)
            if t_q is not None:
                @block.sync
                def _(sync):
                    sync.wait_ge(p.h("qG"), p.val("qG"))
                    sync.dma_start(out=t_q, in_=w1q_dram).then_inc(p.bump("qS", 16), 16)

            if stage < 0.5:
                _finish(p, block)
                return nc, tap_aps

            tr_list = []
            for (src, dst, nrow) in ((w1q_dram, w1qT_dram, H), (w2q_dram, w2qT_dram, H),
                                     (w3q_dram, w3qT_dram, D)):
                for c0 in range(0, src.shape[1], 128):
                    tr_list.append((src, dst, nrow, c0))
            NTR = len(tr_list)

            qG_all = p.val("qG")
            qS_tr0 = p.val("qS")

            @block.sync
            def _(sync):
                sync.wait_ge(p.h("qG"), qG_all)
                for i, (src, dst, nrow, c0) in enumerate(tr_list):
                    if i >= 2:
                        sync.wait_ge(p.h("qG"), qG_all + 16 * (i - 1))
                    sync.dma_start_transpose(out=trbuf[i % 2][:, 0:nrow],
                                             in_=src[:, c0:c0 + 128]).then_inc(p.bump("qS", 16), 16)

            @block.gpsimd
            def _(gpsimd):
                for i, (src, dst, nrow, c0) in enumerate(tr_list):
                    gpsimd.wait_ge(p.h("qS"), qS_tr0 + 16 * (i + 1))
                    gpsimd.dma_start(out=dst[c0:c0 + 128, :],
                                     in_=trbuf[i % 2][:, 0:nrow]).then_inc(p.bump("qG", 16), 16)

            t_w1qt = tap("w1qT", [D, H], BF16)
            if t_w1qt is not None:
                @block.sync
                def _(sync):
                    sync.wait_ge(p.h("qG"), p.val("qG"))
                    sync.dma_start(out=t_w1qt, in_=w1qT_dram).then_inc(p.bump("qS", 16), 16)

        qG_p0_done = p.val("qG")

        if stage < 1:
            _finish(p, block)
            return nc, tap_aps

        p1es = ExitStack()
        with p1es:
            xtile = [p1es.enter_context(nc.sbuf_tensor(f"p1_x{i}", [128, D], F32)) for i in range(2)]
            xntile = [p1es.enter_context(nc.sbuf_tensor(f"p1_xn{i}", [128, D], BF16)) for i in range(2)]
            sqdump = p1es.enter_context(nc.sbuf_tensor("p1_sq", [128, D], F32))
            s1 = p1es.enter_context(nc.sbuf_tensor("p1_s1", [128, 1], F32))
            s2 = p1es.enter_context(nc.sbuf_tensor("p1_s2", [128, 1], F32))
            mean = p1es.enter_context(nc.sbuf_tensor("p1_mean", [128, 1], F32))
            m2 = p1es.enter_context(nc.sbuf_tensor("p1_m2", [128, 1], F32))
            varv = p1es.enter_context(nc.sbuf_tensor("p1_var", [128, 1], F32))
            rstd = p1es.enter_context(nc.sbuf_tensor("p1_rstd", [128, 1], F32))
            bias2 = p1es.enter_context(nc.sbuf_tensor("p1_b2", [128, 1], F32))

            qS_p1 = p.val("qS")
            qG_p1 = p.val("qG")

            for t in range(S // 128):
                bi = t % 2

                @block.sync
                def _(sync, t=t, bi=bi):
                    if t == 0:
                        sync.wait_ge(p.h("qG"), qG_p0_done)
                    if t >= 2:
                        sync.wait_ge(p.h("qG"), qG_p1 + 16 * (t - 1))
                    sync.dma_start(out=xtile[bi][:],
                                   in_=xb[t * 128:(t + 1) * 128, :]).then_inc(p.bump("qS", 16), 16)

                @block.scalar
                def _(scalar, t=t, bi=bi):
                    scalar.wait_ge(p.h("qS"), qS_p1 + 16 * (t + 1))
                    scalar.activation(sqdump[:], xtile[bi][:], AF.Square,
                                      accum_out=s2[:]).then_inc(p.bump("sS", 1), 1)

                @block.vector
                def _(vector, t=t, bi=bi):
                    vector.wait_ge(p.h("qS"), qS_p1 + 16 * (t + 1))
                    vector.tensor_reduce(s1[:], xtile[bi][:], axis=AX.X,
                                         op=ALU.add).then_inc(p.bump("sV", 1), 1)
                    vector.wait_ge(p.h("sV"), p.val("sV"))
                    vector.tensor_scalar(mean[:], s1[:], 1.0 / D, None,
                                         op0=ALU.mult).then_inc(p.bump("sV", 1), 1)
                    vector.wait_ge(p.h("sV"), p.val("sV"))
                    vector.tensor_mul(m2[:], mean[:], mean[:]).then_inc(p.bump("sV", 1), 1)
                    vector.wait_ge(p.h("sS"), p.val("sS"))
                    vector.wait_ge(p.h("sV"), p.val("sV"))
                    vector.scalar_tensor_tensor(varv[:], s2[:], 1.0 / D, m2[:],
                                                op0=ALU.mult, op1=ALU.subtract).then_inc(p.bump("sV", 1), 1)
                    vector.wait_ge(p.h("sV"), p.val("sV"))
                    vector.tensor_scalar(varv[:], varv[:], EPS_LN, None,
                                         op0=ALU.add).then_inc(p.bump("sV", 1), 1)

                @block.scalar
                def _(scalar, t=t, bi=bi):
                    scalar.wait_ge(p.h("sV"), p.val("sV"))
                    scalar.activation(rstd[:], varv[:], AF.Ln).then_inc(p.bump("sS", 1), 1)
                    scalar.wait_ge(p.h("sS"), p.val("sS"))
                    scalar.activation(rstd[:], rstd[:], AF.Exp, scale=-0.5).then_inc(p.bump("sS", 1), 1)

                @block.vector
                def _(vector, t=t, bi=bi):
                    vector.wait_ge(p.h("sS"), p.val("sS"))
                    vector.tensor_scalar(bias2[:], mean[:], rstd[:, 0:1], -1.0,
                                         op0=ALU.mult, op1=ALU.mult).then_inc(p.bump("sV", 1), 1)

                @block.vector
                def _(vector, t=t, bi=bi):
                    vector.wait_ge(p.h("sV"), p.val("sV"))
                    vector.tensor_scalar(xntile[bi][:], xtile[bi][:], rstd[:, 0:1], bias2[:, 0:1],
                                         op0=ALU.mult, op1=ALU.add).then_inc(p.bump("sV", 1), 1)

                @block.gpsimd
                def _(gpsimd, t=t, bi=bi):
                    gpsimd.wait_ge(p.h("sV"), p.val("sV"))
                    gpsimd.dma_start(out=xn_dram[t * 128:(t + 1) * 128, :],
                                     in_=xntile[bi][:]).then_inc(p.bump("qG", 16), 16)

            t_xn = tap("xn", [S, D], BF16)
            if t_xn is not None:
                @block.sync
                def _(sync):
                    sync.wait_ge(p.h("qG"), p.val("qG"))
                    sync.dma_start(out=t_xn, in_=xn_dram).then_inc(p.bump("qS", 16), 16)

        qG_p1_done = p.val("qG")

        if stage < 2:
            _finish(p, block)
            return nc, tap_aps

        p2es = ExitStack()
        with p2es:
            xnT = p2es.enter_context(nc.sbuf_tensor("p2_xnT", [128, DT * S], BF16))
            wqsl = [p2es.enter_context(nc.sbuf_tensor(f"p2_wq{i}", [128, DT * 128], BF16)) for i in range(2)]
            ev = [p2es.enter_context(nc.sbuf_tensor(f"p2_ev{i}", [128, 512], BF16)) for i in range(4)]
            psum_qkv = [p2es.enter_context(nc.psum_tensor(f"p2_ps{i}", [128, 512], F32)) for i in range(2)]

            @block.sync
            def _(sync):
                sync.wait_ge(p.h("qG"), qG_p1_done)
                for dt in range(DT):
                    sync.dma_start_transpose(out=xnT[:, dt * S:(dt + 1) * S],
                                             in_=xn_dram[:, dt * 128:(dt + 1) * 128]).then_inc(p.bump("qS", 16), 16)

            qS_xnT = p.val("qS")

            ft_list = ([("q", ft, qT_dram, S) for ft in range(16)]
                       + [("k", ft, kT_dram, S) for ft in range(16)]
                       + [("v", ft, vT_dram, OWN) for ft in range(16)])
            cnts = [e[3] // 512 for e in ft_list]
            csum = np.cumsum(cnts).tolist()
            NG2 = csum[-1]

            sT_p2 = p.val("sT")
            qG_p2 = p.val("qG")
            sV_p2 = p.val("sV")
            qS_st2 = p.val("qS") + 16 * NG2 * 0

            @block.gpsimd
            def _(gpsimd):
                for i, (kind, ft, dst, ntok) in enumerate(ft_list):
                    col0 = {"q": 0, "k": D, "v": 2 * D}[kind] + ft * 128
                    if i == 0:
                        gpsimd.wait_ge(p.h("qG"), qG_p1_done)
                    if i >= 2:
                        gpsimd.wait_ge(p.h("sT"), sT_p2 + csum[i - 2])
                    src = wqkv[:, col0:col0 + 128].rearrange("(dt pp) c -> pp dt c", pp=128)
                    dstap = wqsl[i % 2][:].rearrange("pp (dt c) -> pp dt c", c=128)
                    gpsimd.dma_start(out=dstap, in_=src).then_inc(p.bump("qG", 16), 16)

            @block.tensor
            def _(tensor):
                k = 0
                tensor.wait_ge(p.h("qS"), qS_xnT)
                for i, (kind, ft, dst, ntok) in enumerate(ft_list):
                    tensor.wait_ge(p.h("qG"), qG_p2 + 16 * (i + 1))
                    for tc in range(ntok // 512):
                        if k >= 2:
                            tensor.wait_ge(p.h("sV"), sV_p2 + k - 1)
                        pb = psum_qkv[k % 2]
                        for dt in range(DT):
                            ins = tensor.matmul(pb[:],
                                          wqsl[i % 2][:, dt * 128:(dt + 1) * 128],
                                          xnT[:, dt * S + tc * 512: dt * S + tc * 512 + 512],
                                          start=(dt == 0), stop=(dt == DT - 1))
                        ins.then_inc(p.bump("sT", 1), 1)
                        k += 1

            qS_ev = p.val("qS")

            @block.vector
            def _(vector):
                k = 0
                for i, (kind, ft, dst, ntok) in enumerate(ft_list):
                    for tc in range(ntok // 512):
                        vector.wait_ge(p.h("sT"), sT_p2 + k + 1)
                        if k >= 4:
                            vector.wait_ge(p.h("qS"), qS_ev + 16 * (k - 3))
                        vector.tensor_copy(ev[k % 4][:], psum_qkv[k % 2][:]).then_inc(p.bump("sV", 1), 1)
                        k += 1

            @block.sync
            def _(sync):
                k = 0
                for i, (kind, ft, dst, ntok) in enumerate(ft_list):
                    for tc in range(ntok // 512):
                        sync.wait_ge(p.h("sV"), sV_p2 + k + 1)
                        sync.dma_start(out=dst[ft * 128:(ft + 1) * 128, tc * 512:(tc + 1) * 512],
                                       in_=ev[k % 4][:]).then_inc(p.bump("qS", 16), 16)
                        k += 1

            t_qT = tap("qT", [D, S], BF16)
            if t_qT is not None:
                @block.gpsimd
                def _(gpsimd):
                    gpsimd.wait_ge(p.h("qS"), p.val("qS"))
                    gpsimd.dma_start(out=t_qT, in_=qT_dram).then_inc(p.bump("qG", 16), 16)

        qS_qkv_done = p.val("qS")

        if stage < 3:
            _finish(p, block)
            return nc, tap_aps

        vt_scope = es.enter_context(ExitStack())
        vt_sb = vt_scope.enter_context(nc.sbuf_tensor("p3_vt", [128, NH * OWN], BF16))

        p3es = ExitStack()
        with p3es:
            kh = p3es.enter_context(nc.sbuf_tensor("p3_kh", [128, S], BF16))
            qh = p3es.enter_context(nc.sbuf_tensor("p3_qh", [128, S], BF16))
            Ebuf = [p3es.enter_context(nc.sbuf_tensor(f"p3_E{i}", [128, S], BF16)) for i in range(2)]
            accT = p3es.enter_context(nc.sbuf_tensor("p3_acc", [128, OWN], F32))
            zcol = [p3es.enter_context(nc.sbuf_tensor(f"p3_z{i}", [128, 1], F32)) for i in range(2)]
            rz = [p3es.enter_context(nc.sbuf_tensor(f"p3_rz{i}", [128, 1], F32)) for i in range(2)]
            cs_sb = p3es.enter_context(nc.sbuf_tensor("p3_cs", [1, OWN], F32))

            t_cs = tap("colsum0", [1, OWN])

            for hd in range(NH):
                hes = ExitStack()
                with hes:
                    psum_s = hes.enter_context(nc.psum_tensor(f"p3_s{hd}", [128, S], F32))

                    sV_pre = p.val("sV")

                    @block.sync
                    def _(sync, hd=hd, sV_pre=sV_pre):
                        sync.wait_ge(p.h("qS"), qS_qkv_done)
                        if hd > 0:
                            sync.wait_ge(p.h("sV"), sV_pre)
                        sync.dma_start(out=kh[:], in_=kT_dram[hd * 128:(hd + 1) * 128, :]).then_inc(p.bump("qS", 16), 16)
                        sync.dma_start(out=qh[:], in_=qT_dram[hd * 128:(hd + 1) * 128, :]).then_inc(p.bump("qS", 16), 16)
                        sync.dma_start(out=vt_sb[:, hd * OWN:(hd + 1) * OWN],
                                       in_=vT_dram[hd * 128:(hd + 1) * 128, :]).then_inc(p.bump("qS", 16), 16)

                    qS_h = p.val("qS")
                    sT_h0 = p.val("sT")
                    sS_h0 = p.val("sS")
                    sV_h0 = p.val("sV")

                    @block.tensor
                    def _(tensor, qS_h=qS_h, sS_h0=sS_h0):
                        tensor.wait_ge(p.h("qS"), qS_h)
                        for qt in range(NQT):
                            if qt >= 1:
                                tensor.wait_ge(p.h("sS"), sS_h0 + 3 * (qt - 1) + 1)
                            for kc in range(S // 512):
                                ins = tensor.matmul(psum_s[:, kc * 512:(kc + 1) * 512],
                                              qh[:, qt * 128:(qt + 1) * 128],
                                              kh[:, kc * 512:(kc + 1) * 512],
                                              start=True, stop=True)
                            ins.then_inc(p.bump("sT", 1), 1)

                    @block.scalar
                    def _(scalar, sT_h0=sT_h0, sV_h0=sV_h0):
                        for qt in range(NQT):
                            scalar.wait_ge(p.h("sT"), sT_h0 + qt + 1)
                            if qt >= 2:
                                scalar.wait_ge(p.h("sV"), sV_h0 + qt)
                            scalar.activation(Ebuf[qt % 2][:], psum_s[:], AF.Exp,
                                              scale=INVSQ,
                                              accum_out=zcol[qt % 2][:]).then_inc(p.bump("sS", 1), 1)
                            scalar.wait_ge(p.h("sS"), p.val("sS"))
                            scalar.activation(rz[qt % 2][:], zcol[qt % 2][:], AF.Ln).then_inc(p.bump("sS", 1), 1)
                            scalar.wait_ge(p.h("sS"), p.val("sS"))
                            scalar.activation(rz[qt % 2][:], rz[qt % 2][:], AF.Exp,
                                              scale=-1.0).then_inc(p.bump("sS", 1), 1)

                    @block.vector
                    def _(vector, sS_h0=sS_h0):
                        vector.memset(accT[:], 0.0).then_inc(p.bump("sV", 1), 1)
                        vector.wait_ge(p.h("sV"), p.val("sV"))
                        for qt in range(NQT):
                            vector.wait_ge(p.h("sS"), sS_h0 + 3 * (qt + 1))
                            vector.scalar_tensor_tensor(accT[:], Ebuf[qt % 2][:, 0:OWN],
                                                        rz[qt % 2][:, 0:1], accT[:],
                                                        op0=ALU.mult, op1=ALU.add).then_inc(p.bump("sV", 1), 1)

                    sV_acc_done = p.val("sV")

                hes2 = ExitStack()
                with hes2:
                    psum_bc = hes2.enter_context(nc.psum_tensor(f"p3_bc{hd}", [128, OWN], F32))

                    @block.tensor
                    def _(tensor, sV_acc_done=sV_acc_done):
                        tensor.wait_ge(p.h("sV"), sV_acc_done)
                        for c in range(OWN // 512):
                            ins = tensor.matmul(psum_bc[:, c * 512:(c + 1) * 512], ones_sq[:],
                                          accT[:, c * 512:(c + 1) * 512], start=True, stop=True)
                        ins.then_inc(p.bump("sT", 1), 1)

                    if t_cs is not None and hd == 0:
                        @block.scalar
                        def _(scalar):
                            scalar.wait_ge(p.h("sT"), p.val("sT"))
                            scalar.copy(cs_sb[:], psum_bc[0:1, :]).then_inc(p.bump("sS", 1), 1)

                        @block.sync
                        def _(sync):
                            sync.wait_ge(p.h("sS"), p.val("sS"))
                            sync.dma_start(out=t_cs, in_=cs_sb[:]).then_inc(p.bump("qS", 16), 16)

                    @block.vector
                    def _(vector, hd=hd, qS_h=qS_h):
                        vector.wait_ge(p.h("sT"), p.val("sT"))
                        vector.wait_ge(p.h("qS"), qS_h)
                        vector.tensor_tensor(vt_sb[:, hd * OWN:(hd + 1) * OWN],
                                             vt_sb[:, hd * OWN:(hd + 1) * OWN],
                                             psum_bc[:], op=ALU.mult).then_inc(p.bump("sV", 1), 1)

        sV_p3_done = p.val("sV")

        if stage < 4:
            _finish(p, block)
            return nc, tap_aps

        p4es = ExitStack()
        with p4es:
            wo_sb = p4es.enter_context(nc.sbuf_tensor("p4_wo", [128, NH * D], BF16))
            psum_o = [p4es.enter_context(nc.psum_tensor(f"p4_ps{i}", [128, 512], F32)) for i in range(2)]
            oev = [p4es.enter_context(nc.sbuf_tensor(f"p4_ev{i}", [128, 512], BF16)) for i in range(4)]

            qG_wo0 = p.val("qG")

            @block.gpsimd
            def _(gpsimd):
                gpsimd.wait_ge(p.h("sV"), sV_p3_done)
                for hd in range(NH):
                    gpsimd.dma_start(out=wo_sb[:, hd * D:(hd + 1) * D],
                                     in_=wo[hd * 128:(hd + 1) * 128, :]).then_inc(p.bump("qG", 16), 16)

            qG_wo = p.val("qG")
            sT_p4 = p.val("sT")
            sV_p4 = p.val("sV")
            NG4 = DT * (OWN // 512)

            @block.tensor
            def _(tensor):
                k = 0
                tensor.wait_ge(p.h("qG"), qG_wo)
                tensor.wait_ge(p.h("sV"), sV_p3_done)
                for dm in range(DT):
                    for tc in range(OWN // 512):
                        if k >= 2:
                            tensor.wait_ge(p.h("sV"), sV_p4 + k - 1)
                        pb = psum_o[k % 2]
                        for hd in range(NH):
                            ins = tensor.matmul(pb[:],
                                          wo_sb[:, hd * D + dm * 128: hd * D + dm * 128 + 128],
                                          vt_sb[:, hd * OWN + tc * 512: hd * OWN + tc * 512 + 512],
                                          start=(hd == 0), stop=(hd == NH - 1))
                        ins.then_inc(p.bump("sT", 1), 1)
                        k += 1

            qS_p4 = p.val("qS")

            @block.vector
            def _(vector):
                k = 0
                for dm in range(DT):
                    for tc in range(OWN // 512):
                        vector.wait_ge(p.h("sT"), sT_p4 + k + 1)
                        if k >= 4:
                            vector.wait_ge(p.h("qS"), qS_p4 + 16 * (k - 3))
                        vector.tensor_copy(oev[k % 4][:], psum_o[k % 2][:]).then_inc(p.bump("sV", 1), 1)
                        k += 1

            @block.sync
            def _(sync):
                k = 0
                for dm in range(DT):
                    for tc in range(OWN // 512):
                        sync.wait_ge(p.h("sV"), sV_p4 + k + 1)
                        sync.dma_start(out=attnT_dram[dm * 128:(dm + 1) * 128, tc * 512:(tc + 1) * 512],
                                       in_=oev[k % 4][:]).then_inc(p.bump("qS", 16), 16)
                        k += 1

            t_attnT = tap("attnT", [D, OWN], BF16)
            if t_attnT is not None:
                @block.gpsimd
                def _(gpsimd):
                    gpsimd.wait_ge(p.h("qS"), p.val("qS"))
                    gpsimd.dma_start(out=t_attnT, in_=attnT_dram).then_inc(p.bump("qG", 16), 16)

        vt_scope.close()
        qS_attn_done = p.val("qS")

        if stage < 5:
            _finish(p, block)
            return nc, tap_aps

        TOKC = 512
        NBLK = 256
        HGR = 2
        NGRP = (HT + HGR - 1) // HGR
        p5es = ExitStack()
        with p5es:
            hid = p5es.enter_context(nc.sbuf_tensor("p5_hid", [128, HT * TOKC], BF16))
            xnc = p5es.enter_context(nc.sbuf_tensor("p5_xnc", [128, DT * TOKC], BF16))
            w1g = [p5es.enter_context(nc.sbuf_tensor(f"p5_w1g{i}", [128, DT * 128 * HGR], BF16)) for i in range(2)]
            w2g = [p5es.enter_context(nc.sbuf_tensor(f"p5_w2g{i}", [128, DT * 128 * HGR], BF16)) for i in range(2)]
            w3sg = [p5es.enter_context(nc.sbuf_tensor(f"p5_w3sg{i}", [128, 6 * D], BF16)) for i in range(2)]
            out_acc = [p5es.enter_context(nc.sbuf_tensor(f"p5_oa{i}", [128, D], F32)) for i in range(2)]
            gate = [p5es.enter_context(nc.sbuf_tensor(f"p5_gate{i}", [128, 512], F32)) for i in range(2)]
            xmrg = [p5es.enter_context(nc.sbuf_tensor(f"p5_xm{i}", [128, D], F32)) for i in range(2)]
            amrg = [p5es.enter_context(nc.sbuf_tensor(f"p5_am{i}", [128, D], BF16)) for i in range(2)]
            omrg = [p5es.enter_context(nc.sbuf_tensor(f"p5_om{i}", [128, D], F32)) for i in range(2)]

            for ck in range(OWN // TOKC):
                tok0 = ck * TOKC

                sT_prev_all = p.val("sT")
                sV_prev_mrg = p.val("sV")

                @block.sync
                def _(sync, ck=ck, tok0=tok0, sT_prev_all=sT_prev_all):
                    sync.wait_ge(p.h("qS"), qS_attn_done)
                    if ck > 0:
                        sync.wait_ge(p.h("sT"), sT_prev_all)
                    for dt in range(DT):
                        sync.dma_start_transpose(out=xnc[:, dt * TOKC:(dt + 1) * TOKC],
                                                 in_=xn_dram[tok0:tok0 + TOKC, dt * 128:(dt + 1) * 128]).then_inc(p.bump("qS", 16), 16)

                qS_xnc = p.val("qS")

                NHALF = TOKC // 512
                grp_nht = [min(HGR, HT - g * HGR) for g in range(NGRP)]
                grp_mm = [2 * n * NHALF for n in grp_nht]
                grp_mm_csum = np.cumsum(grp_mm).tolist()

                sT_pr = p.val("sT")
                qG_w12 = p.val("qG")
                sS_pr = p.val("sS")
                sV_pr = p.val("sV")

                @block.gpsimd
                def _(gpsimd, ck=ck, sT_pr=sT_pr):
                    if ck == 0:
                        gpsimd.wait_ge(p.h("qS"), qS_attn_done)
                    for g in range(NGRP):
                        nht = grp_nht[g]
                        if g < 2:
                            gpsimd.wait_ge(p.h("sT"), sT_pr)
                        else:
                            gpsimd.wait_ge(p.h("sT"), sT_pr + grp_mm_csum[g - 2])
                        for wsl, wdr in ((w1g, w1qT_dram), (w2g, w2qT_dram)):
                            for dt in range(DT):
                                gpsimd.dma_start(
                                    out=wsl[g % 2][:, dt * 128 * HGR: dt * 128 * HGR + nht * 128],
                                    in_=wdr[dt * 128:(dt + 1) * 128,
                                            g * HGR * 128: g * HGR * 128 + nht * 128]).then_inc(p.bump("qG", 16), 16)

                pres = ExitStack()
                with pres:
                    psx1 = [pres.enter_context(nc.psum_tensor(f"p5_x1_{ck}_{i}", [128, 512], F32)) for i in range(2)]
                    psx2 = [pres.enter_context(nc.psum_tensor(f"p5_x2_{ck}_{i}", [128, 512], F32)) for i in range(2)]

                    @block.tensor
                    def _(tensor, ck=ck, qS_xnc=qS_xnc, qG_w12=qG_w12, sV_pr=sV_pr, sV_prev_mrg=sV_prev_mrg):
                        npr = 0
                        tensor.wait_ge(p.h("qS"), qS_xnc)
                        if ck > 0:
                            tensor.wait_ge(p.h("sV"), sV_prev_mrg)
                        for g in range(NGRP):
                            nht = grp_nht[g]
                            tensor.wait_ge(p.h("qG"), qG_w12 + 16 * 2 * DT * (g + 1))
                            for hi in range(nht):
                                for half in range(NHALF):
                                    if npr >= 2:
                                        tensor.wait_ge(p.h("sV"), sV_pr + npr - 1)
                                    for dt in range(DT):
                                        ins = tensor.matmul(psx1[npr % 2][:],
                                                      w1g[g % 2][:, dt * 128 * HGR + hi * 128: dt * 128 * HGR + hi * 128 + 128],
                                                      xnc[:, dt * TOKC + half * 512: dt * TOKC + half * 512 + 512],
                                                      start=(dt == 0), stop=(dt == DT - 1))
                                    ins.then_inc(p.bump("sT", 1), 1)
                                    for dt in range(DT):
                                        ins = tensor.matmul(psx2[npr % 2][:],
                                                      w2g[g % 2][:, dt * 128 * HGR + hi * 128: dt * 128 * HGR + hi * 128 + 128],
                                                      xnc[:, dt * TOKC + half * 512: dt * TOKC + half * 512 + 512],
                                                      start=(dt == 0), stop=(dt == DT - 1))
                                    ins.then_inc(p.bump("sT", 1), 1)
                                    npr += 1

                    @block.scalar
                    def _(scalar, ck=ck, sT_pr=sT_pr):
                        k = 0
                        for g in range(NGRP):
                            for hi in range(grp_nht[g]):
                                for half in range(NHALF):
                                    scalar.wait_ge(p.h("sT"), sT_pr + 2 * k + 1)
                                    scalar.activation(gate[k % 2][:], psx1[k % 2][:],
                                                      AF.Sigmoid).then_inc(p.bump("sS", 1), 1)
                                    k += 1

                    @block.vector
                    def _(vector, ck=ck, sT_pr=sT_pr, sS_pr=sS_pr, sT_prev_all=sT_prev_all):
                        k = 0
                        vector.wait_ge(p.h("sT"), sT_prev_all)
                        for g in range(NGRP):
                            for hi in range(grp_nht[g]):
                                ht = g * HGR + hi
                                for half in range(NHALF):
                                    vector.wait_ge(p.h("sS"), sS_pr + k + 1)
                                    vector.wait_ge(p.h("sT"), sT_pr + 2 * k + 2)
                                    vector.tensor_tensor(hid[:, ht * TOKC + half * 512: ht * TOKC + half * 512 + 512],
                                                         gate[k % 2][:], psx2[k % 2][:],
                                                         op=ALU.mult).then_inc(p.bump("sV", 1), 1)
                                    k += 1

                sV_hid_done = p.val("sV")

                if ck == 0:
                    t_hid = tap("hid0", [128, HT * TOKC], BF16)
                    if t_hid is not None:
                        @block.gpsimd
                        def _(gpsimd):
                            gpsimd.wait_ge(p.h("sV"), sV_hid_done)
                            gpsimd.dma_start(out=t_hid, in_=hid[:]).then_inc(p.bump("qG", 16), 16)

                SGH = 6
                sg_list = [(s * SGH, min(SGH, HT - s * SGH)) for s in range((HT + SGH - 1) // SGH)]
                NSG = len(sg_list)

                for blk in range(TOKC // NBLK):
                    btok = tok0 + blk * NBLK
                    sV_mrg_prev = p.val("sV")
                    ces = ExitStack()
                    with ces:
                        pso = [ces.enter_context(nc.psum_tensor(f"p5_o_{ck}_{blk}_{i}", [128, D], F32)) for i in range(2)]

                        qG_w3 = p.val("qG")
                        sT_c = p.val("sT")
                        sV_ev0 = p.val("sV")

                        @block.gpsimd
                        def _(gpsimd, ck=ck, blk=blk, sT_c=sT_c):
                            for s, (ht0, nht) in enumerate(sg_list):
                                if s < 2:
                                    gpsimd.wait_ge(p.h("sT"), sT_c)
                                else:
                                    gpsimd.wait_ge(p.h("sT"), sT_c + 2 * (s - 1))
                                gpsimd.dma_start(
                                    out=w3sg[s % 2][:, 0:nht * D].rearrange("pp (t d) -> pp t d", d=D),
                                    in_=w3qT_dram[ht0 * 128:(ht0 + nht) * 128, :].rearrange("(t pp) d -> pp t d", pp=128)
                                ).then_inc(p.bump("qG", 16), 16)

                        @block.tensor
                        def _(tensor, ck=ck, blk=blk, sV_hid_done=sV_hid_done, qG_w3=qG_w3, sV_ev0=sV_ev0, sV_mrg_prev=sV_mrg_prev):
                            tensor.wait_ge(p.h("sV"), sV_hid_done)
                            if not (ck == 0 and blk == 0):
                                tensor.wait_ge(p.h("sV"), sV_mrg_prev)
                            for s, (ht0, nht) in enumerate(sg_list):
                                tensor.wait_ge(p.h("qG"), qG_w3 + 16 * (s + 1))
                                if s >= 1:
                                    tensor.wait_ge(p.h("sV"), sV_ev0 + 2 * s)
                                for tt in range(NBLK // 128):
                                    for hi in range(nht):
                                        ht = ht0 + hi
                                        for dmc in range(D // 512):
                                            ins = tensor.matmul(
                                                pso[tt][:, dmc * 512:(dmc + 1) * 512],
                                                hid[:, ht * TOKC + blk * NBLK + tt * 128: ht * TOKC + blk * NBLK + tt * 128 + 128],
                                                w3sg[s % 2][:, hi * D + dmc * 512: hi * D + dmc * 512 + 512],
                                                start=(hi == 0), stop=(hi == nht - 1))
                                    ins.then_inc(p.bump("sT", 1), 1)

                        @block.vector
                        def _(vector, ck=ck, blk=blk, sT_c=sT_c, sV_ev0=sV_ev0):
                            for s in range(NSG):
                                for tt in range(NBLK // 128):
                                    vector.wait_ge(p.h("sT"), sT_c + 2 * s + tt + 1)
                                    if s == 0:
                                        vector.tensor_copy(out_acc[tt][:], pso[tt][:]).then_inc(p.bump("sV", 1), 1)
                                    else:
                                        vector.tensor_tensor(out_acc[tt][:], out_acc[tt][:], pso[tt][:],
                                                             op=ALU.add).then_inc(p.bump("sV", 1), 1)

                        sV_acc_out = p.val("sV")

                        if ck == 0 and blk == 0:
                            t_oa = tap("ffn00", [128, D])
                            if t_oa is not None:
                                @block.gpsimd
                                def _(gpsimd, sV_acc_out=sV_acc_out):
                                    gpsimd.wait_ge(p.h("sV"), sV_acc_out)
                                    gpsimd.dma_start(out=t_oa, in_=out_acc[0][:]).then_inc(p.bump("qG", 16), 16)

                        @block.sync
                        def _(sync, ck=ck, blk=blk, btok=btok, sV_mrg_prev=sV_mrg_prev):
                            if not (ck == 0 and blk == 0):
                                sync.wait_ge(p.h("sV"), sV_mrg_prev)
                            for tt in range(NBLK // 128):
                                sync.dma_start(out=xmrg[tt][:],
                                               in_=xb[btok + tt * 128: btok + tt * 128 + 128, :]).then_inc(p.bump("qS", 16), 16)
                                sync.dma_start_transpose(out=amrg[tt][:],
                                                         in_=attnT_dram[:, btok + tt * 128: btok + tt * 128 + 128]).then_inc(p.bump("qS", 16), 16)

                        qS_mrg = p.val("qS")

                        @block.vector
                        def _(vector, ck=ck, blk=blk, qS_mrg=qS_mrg, sV_acc_out=sV_acc_out):
                            vector.wait_ge(p.h("sV"), sV_acc_out)
                            vector.wait_ge(p.h("qS"), qS_mrg)
                            for tt in range(NBLK // 128):
                                vector.tensor_add(omrg[tt][:], out_acc[tt][:], xmrg[tt][:]).then_inc(p.bump("sV", 1), 1)
                                vector.wait_ge(p.h("sV"), p.val("sV"))
                                vector.tensor_add(omrg[tt][:], omrg[tt][:], amrg[tt][:]).then_inc(p.bump("sV", 1), 1)

                        sV_mrg = p.val("sV")

                        @block.sync
                        def _(sync, ck=ck, blk=blk, btok=btok, sV_mrg=sV_mrg):
                            sync.wait_ge(p.h("sV"), sV_mrg)
                            for tt in range(NBLK // 128):
                                sync.dma_start(out=out[btok + tt * 128: btok + tt * 128 + 128, :],
                                               in_=omrg[tt][:]).then_inc(p.bump("qS", 16), 16)

        @block.sync
        def _(sync):
            sync.wait_ge(p.h("qS"), p.val("qS"))
            sync.wait_ge(p.h("qG"), p.val("qG"))

    return nc, tap_aps


def kernel(L0_attn_x, L0_attn_w_qkv, L0_attn_w_o, L0_ffn_w1, L0_ffn_w2, L0_ffn_w3,
           _taps=(), _stage=99.0):
    x = np.ascontiguousarray(np.asarray(L0_attn_x, dtype=np.float32))
    wqkv = np.ascontiguousarray(np.asarray(L0_attn_w_qkv, dtype=np.float32))
    wo = np.ascontiguousarray(np.asarray(L0_attn_w_o, dtype=np.float32))
    w1 = np.ascontiguousarray(np.asarray(L0_ffn_w1, dtype=np.float32))
    w2 = np.ascontiguousarray(np.asarray(L0_ffn_w2, dtype=np.float32))
    w3 = np.ascontiguousarray(np.asarray(L0_ffn_w3, dtype=np.float32))

    nc, tap_aps = build_program(taps=_taps, stage=_stage)

    in_maps = []
    for c in range(8):
        b, g = divmod(c, 2)
        own = x[b, g * OWN:(g + 1) * OWN, :]
        other = x[b, (1 - g) * OWN:(2 - g) * OWN, :]
        xb_c = np.ascontiguousarray(np.concatenate([own, other], axis=0))
        in_maps.append({"xb": xb_c, "wqkv": wqkv, "wo": wo,
                        "w1": w1, "w2": w2, "w3": w3})

    res = run_bass_kernel_spmd(nc, in_maps, core_ids=list(range(8)))

    outv = np.empty((B, S, D), dtype=np.float32)
    for c in range(8):
        b, g = divmod(c, 2)
        outv[b, g * OWN:(g + 1) * OWN, :] = res.results[c]["out"]
    if _taps:
        kernel._tap_results = [{k: r["tap_" + k] for k in _taps} for r in res.results]
    return outv


# version 03
# speedup vs baseline: 1.1262x; 1.1262x over previous
import numpy as np
import ml_dtypes
from contextlib import ExitStack

import concourse.bass as bass
import concourse.mybir as mybir
from concourse.bass_utils import run_bass_kernel_spmd

F32 = mybir.dt.float32
BF16 = mybir.dt.bfloat16
AF = mybir.ActivationFunctionType
ALU = mybir.AluOpType
AX = mybir.AxisListType

B = 4
S = 4096
D = 2048
NH = 16
DH = 128
H = 5632
OWN = 2048
DT = D // 128
HT = H // 128
NQT = S // 128
EPS_LN = 1e-5
EPS_Q = 1e-8
W_ELEMS = H * D
INVSQ = float(1.0 / np.sqrt(DH))


class Prog:
    def __init__(self, nc, es):
        self.nc = nc
        self.sems = {}
        for name in ("qS", "qG", "sT", "sV", "sS"):
            self.sems[name] = [es.enter_context(nc.semaphore(name)), 0]

    def bump(self, name, n):
        s = self.sems[name]
        s[1] += n
        return s[0]

    def val(self, name):
        return self.sems[name][1]

    def h(self, name):
        return self.sems[name][0]


def _finish(p, block):
    @block.sync
    def _(sync):
        sync.wait_ge(p.h("qS"), p.val("qS"))
        sync.wait_ge(p.h("qG"), p.val("qG"))


def build_program(taps=(), stage=99.0):
    nc = bass.Bass("TRN2", target_bir_lowering=False)

    xb = nc.dram_tensor("xb", [S, D], F32, kind="ExternalInput").ap()
    wqkv = nc.dram_tensor("wqkv", [D, 3 * D], F32, kind="ExternalInput").ap()
    wo = nc.dram_tensor("wo", [D, D], F32, kind="ExternalInput").ap()
    w1 = nc.dram_tensor("w1", [H, D], F32, kind="ExternalInput").ap()
    w2 = nc.dram_tensor("w2", [H, D], F32, kind="ExternalInput").ap()
    w3 = nc.dram_tensor("w3", [D, H], F32, kind="ExternalInput").ap()
    out = nc.dram_tensor("out", [OWN, D], F32, kind="ExternalOutput").ap()

    tap_aps = {}

    def tap(name, shape, dtype=F32):
        if name in taps:
            tap_aps[name] = nc.dram_tensor("tap_" + name, shape, dtype,
                                           kind="ExternalOutput").ap()
            return tap_aps[name]
        return None

    xn_dram = nc.dram_tensor("xn_dram", [S, D], BF16, kind="Internal").ap()
    qT_dram = nc.dram_tensor("qT_dram", [D, S], BF16, kind="Internal").ap()
    kT_dram = nc.dram_tensor("kT_dram", [D, S], BF16, kind="Internal").ap()
    vT_dram = nc.dram_tensor("vT_dram", [D, OWN], BF16, kind="Internal").ap()
    w1q_dram = nc.dram_tensor("w1q_dram", [H, D], BF16, kind="Internal").ap()
    w2q_dram = nc.dram_tensor("w2q_dram", [H, D], BF16, kind="Internal").ap()
    w3q_dram = nc.dram_tensor("w3q_dram", [D, H], BF16, kind="Internal").ap()
    w1qT_dram = nc.dram_tensor("w1qT_dram", [D, H], BF16, kind="Internal").ap()
    w2qT_dram = nc.dram_tensor("w2qT_dram", [D, H], BF16, kind="Internal").ap()
    w3qT_dram = nc.dram_tensor("w3qT_dram", [H, D], BF16, kind="Internal").ap()
    attnT_dram = nc.dram_tensor("attnT_dram", [D, OWN], BF16, kind="Internal").ap()

    ones_col_f32 = nc.inline_tensor(np.ones((128, 1), dtype=np.float32), name="c_ones_col")
    ones_sq_f32 = nc.inline_tensor(np.ones((128, 128), dtype=np.float32), name="c_ones_sq")

    es = ExitStack()
    with es:
        block = es.enter_context(nc.Block())
        p = Prog(nc, es)

        ones_c = es.enter_context(nc.sbuf_tensor("ones_c", [128, 1], F32))
        ones_sq = es.enter_context(nc.sbuf_tensor("ones_sq", [128, 128], F32))
        h_sb = es.enter_context(nc.sbuf_tensor("h_sb", [128, 4], F32))
        nh_sb = es.enter_context(nc.sbuf_tensor("nh_sb", [128, 4], F32))

        @block.sync
        def _(sync):
            sync.dma_start(out=ones_c[:], in_=ones_col_f32.ap()).then_inc(p.bump("qS", 16), 16)
            sync.dma_start(out=ones_sq[:], in_=ones_sq_f32.ap()).then_inc(p.bump("qS", 16), 16)

        ws = [(w1, w1q_dram), (w2, w2q_dram), (w3, w3q_dram)]
        w_tiles = []
        for mi, (wa, _) in enumerate(ws):
            for r0 in range(0, wa.shape[0], 128):
                w_tiles.append((mi, r0))
        NT0 = len(w_tiles)

        p0es = ExitStack()
        with p0es:
            wbuf = [p0es.enter_context(nc.sbuf_tensor(f"p0_w{i}", [128, H], F32)) for i in range(2)]
            asum = p0es.enter_context(nc.sbuf_tensor("asum", [128, NT0], F32))
            atot = p0es.enter_context(nc.sbuf_tensor("atot", [128, 4], F32))
            qtile = [p0es.enter_context(nc.sbuf_tensor(f"p0_q{i}", [128, H], BF16)) for i in range(2)]
            t1 = p0es.enter_context(nc.sbuf_tensor("p0_t1", [128, H], BF16))
            t2 = p0es.enter_context(nc.sbuf_tensor("p0_t2", [128, H], BF16))
            trbuf = [p0es.enter_context(nc.sbuf_tensor(f"p0_tr{i}", [128, H], BF16)) for i in range(2)]
            psum_h = p0es.enter_context(nc.psum_tensor("psum_h", [128, 4], F32))

            qS0 = p.val("qS")
            sV0 = p.val("sV")

            @block.sync
            def _(sync):
                for i, (mi, r0) in enumerate(w_tiles):
                    wa = ws[mi][0]
                    if i >= 2:
                        sync.wait_ge(p.h("sV"), sV0 + i - 1)
                    sync.dma_start(out=wbuf[i % 2][:, 0:wa.shape[1]],
                                   in_=wa[r0:r0 + 128, :]).then_inc(p.bump("qS", 16), 16)

            @block.vector
            def _(vector):
                for i, (mi, r0) in enumerate(w_tiles):
                    ncols = ws[mi][0].shape[1]
                    vector.wait_ge(p.h("qS"), qS0 + 16 * (i + 1))
                    vector.tensor_reduce(asum[:, i:i + 1], wbuf[i % 2][:, 0:ncols], axis=AX.X,
                                         op=ALU.add, apply_absolute_value=True).then_inc(p.bump("sV", 1), 1)
                vector.wait_ge(p.h("sV"), p.val("sV"))
                n1 = 0
                for mi in range(3):
                    cnt = sum(1 for (m, _) in w_tiles if m == mi)
                    ins = vector.tensor_reduce(atot[:, mi:mi + 1], asum[:, n1:n1 + cnt],
                                               axis=AX.X, op=ALU.add)
                    if mi == 2:
                        ins.then_inc(p.bump("sV", 1), 1)
                    n1 += cnt
                vector.wait_ge(p.h("sV"), p.val("sV"))

            @block.tensor
            def _(tensor):
                tensor.wait_ge(p.h("sV"), p.val("sV"))
                tensor.wait_ge(p.h("qS"), 32)
                for mi in range(3):
                    ins = tensor.matmul(psum_h[:, mi:mi + 1], ones_sq[:], atot[:, mi:mi + 1],
                                  start=True, stop=True)
                ins.then_inc(p.bump("sT", 1), 1)

            @block.vector
            def _(vector):
                vector.wait_ge(p.h("sT"), p.val("sT"))
                vector.tensor_scalar(h_sb[:, 0:3], psum_h[:, 0:3], 1.0 / W_ELEMS, None,
                                     op0=ALU.mult).then_inc(p.bump("sV", 1), 1)
                vector.wait_ge(p.h("sV"), p.val("sV"))
                vector.tensor_scalar(h_sb[:, 0:3], h_sb[:, 0:3], EPS_Q, 0.5,
                                     op0=ALU.add, op1=ALU.mult).then_inc(p.bump("sV", 1), 1)
                vector.wait_ge(p.h("sV"), p.val("sV"))
                vector.tensor_scalar(nh_sb[:, 0:3], h_sb[:, 0:3], -1.0, None,
                                     op0=ALU.mult).then_inc(p.bump("sV", 1), 1)

            t_at = tap("atot", [128, 4])
            if t_at is not None:
                @block.sync
                def _(sync):
                    sync.wait_ge(p.h("sV"), p.val("sV"))
                    sync.dma_start(out=t_at, in_=atot[:]).then_inc(p.bump("qS", 16), 16)
            t_as = tap("asum", [128, NT0])
            if t_as is not None:
                @block.sync
                def _(sync):
                    sync.wait_ge(p.h("sV"), p.val("sV"))
                    sync.dma_start(out=t_as, in_=asum[:]).then_inc(p.bump("qS", 16), 16)

            t_gamma = tap("hthr", [1, 4])
            if t_gamma is not None:
                @block.sync
                def _(sync):
                    sync.wait_ge(p.h("sV"), p.val("sV"))
                    sync.dma_start(out=t_gamma, in_=h_sb[0:1, :]).then_inc(p.bump("qS", 16), 16)

            if stage < 0.2:
                _finish(p, block)
                return nc, tap_aps

            sVq = p.val("sV")
            qSq = p.val("qS")
            qGq = p.val("qG")

            @block.sync
            def _(sync):
                sync.wait_ge(p.h("sV"), p.val("sV"))
                for i, (mi, r0) in enumerate(w_tiles):
                    wa = ws[mi][0]
                    if i >= 2:
                        sync.wait_ge(p.h("sV"), sVq + i - 1)
                    sync.dma_start(out=wbuf[i % 2][:, 0:wa.shape[1]],
                                   in_=wa[r0:r0 + 128, :]).then_inc(p.bump("qS", 16), 16)

            @block.vector
            def _(vector):
                for i, (mi, r0) in enumerate(w_tiles):
                    ncols = ws[mi][0].shape[1]
                    vector.wait_ge(p.h("qS"), qSq + 16 * (i + 1))
                    if i >= 2:
                        vector.wait_ge(p.h("qG"), qGq + 16 * (i - 1))
                    vector.tensor_scalar(t1[:, 0:ncols], wbuf[i % 2][:, 0:ncols],
                                         h_sb[:, mi:mi + 1], None, op0=ALU.is_gt)
                    vector.tensor_scalar(t2[:, 0:ncols], wbuf[i % 2][:, 0:ncols],
                                         nh_sb[:, mi:mi + 1], None, op0=ALU.is_lt)
                    vector.tensor_tensor(qtile[i % 2][:, 0:ncols], t1[:, 0:ncols], t2[:, 0:ncols],
                                         op=ALU.subtract).then_inc(p.bump("sV", 1), 1)

            @block.gpsimd
            def _(gpsimd):
                for i, (mi, r0) in enumerate(w_tiles):
                    qd = ws[mi][1]
                    gpsimd.wait_ge(p.h("sV"), sVq + i + 1)
                    gpsimd.dma_start(out=qd[r0:r0 + 128, :],
                                     in_=qtile[i % 2][:, 0:qd.shape[1]]).then_inc(p.bump("qG", 16), 16)

            t_q = tap("w1q", [H, D], BF16)
            if t_q is not None:
                @block.sync
                def _(sync):
                    sync.wait_ge(p.h("qG"), p.val("qG"))
                    sync.dma_start(out=t_q, in_=w1q_dram).then_inc(p.bump("qS", 16), 16)

            if stage < 0.5:
                _finish(p, block)
                return nc, tap_aps

            tr_list = []
            for (src, dst, nrow) in ((w1q_dram, w1qT_dram, H), (w2q_dram, w2qT_dram, H),
                                     (w3q_dram, w3qT_dram, D)):
                for c0 in range(0, src.shape[1], 128):
                    tr_list.append((src, dst, nrow, c0))
            NTR = len(tr_list)

            qG_all = p.val("qG")
            qS_tr0 = p.val("qS")

            @block.sync
            def _(sync):
                sync.wait_ge(p.h("qG"), qG_all)
                for i, (src, dst, nrow, c0) in enumerate(tr_list):
                    if i >= 2:
                        sync.wait_ge(p.h("qG"), qG_all + 16 * (i - 1))
                    sync.dma_start_transpose(out=trbuf[i % 2][:, 0:nrow],
                                             in_=src[:, c0:c0 + 128]).then_inc(p.bump("qS", 16), 16)

            @block.gpsimd
            def _(gpsimd):
                for i, (src, dst, nrow, c0) in enumerate(tr_list):
                    gpsimd.wait_ge(p.h("qS"), qS_tr0 + 16 * (i + 1))
                    gpsimd.dma_start(out=dst[c0:c0 + 128, :],
                                     in_=trbuf[i % 2][:, 0:nrow]).then_inc(p.bump("qG", 16), 16)

            t_w1qt = tap("w1qT", [D, H], BF16)
            if t_w1qt is not None:
                @block.sync
                def _(sync):
                    sync.wait_ge(p.h("qG"), p.val("qG"))
                    sync.dma_start(out=t_w1qt, in_=w1qT_dram).then_inc(p.bump("qS", 16), 16)

        qG_p0_done = p.val("qG")

        if stage < 1:
            _finish(p, block)
            return nc, tap_aps

        p1es = ExitStack()
        with p1es:
            xtile = [p1es.enter_context(nc.sbuf_tensor(f"p1_x{i}", [128, D], F32)) for i in range(2)]
            xntile = [p1es.enter_context(nc.sbuf_tensor(f"p1_xn{i}", [128, D], BF16)) for i in range(2)]
            sqdump = p1es.enter_context(nc.sbuf_tensor("p1_sq", [128, D], F32))
            s1 = p1es.enter_context(nc.sbuf_tensor("p1_s1", [128, 1], F32))
            s2 = p1es.enter_context(nc.sbuf_tensor("p1_s2", [128, 1], F32))
            mean = p1es.enter_context(nc.sbuf_tensor("p1_mean", [128, 1], F32))
            m2 = p1es.enter_context(nc.sbuf_tensor("p1_m2", [128, 1], F32))
            varv = p1es.enter_context(nc.sbuf_tensor("p1_var", [128, 1], F32))
            rstd = p1es.enter_context(nc.sbuf_tensor("p1_rstd", [128, 1], F32))
            bias2 = p1es.enter_context(nc.sbuf_tensor("p1_b2", [128, 1], F32))

            qS_p1 = p.val("qS")
            qG_p1 = p.val("qG")

            for t in range(S // 128):
                bi = t % 2

                @block.sync
                def _(sync, t=t, bi=bi):
                    if t == 0:
                        sync.wait_ge(p.h("qG"), qG_p0_done)
                    if t >= 2:
                        sync.wait_ge(p.h("qG"), qG_p1 + 16 * (t - 1))
                    sync.dma_start(out=xtile[bi][:],
                                   in_=xb[t * 128:(t + 1) * 128, :]).then_inc(p.bump("qS", 16), 16)

                @block.scalar
                def _(scalar, t=t, bi=bi):
                    scalar.wait_ge(p.h("qS"), qS_p1 + 16 * (t + 1))
                    scalar.activation(sqdump[:], xtile[bi][:], AF.Square,
                                      accum_out=s2[:]).then_inc(p.bump("sS", 1), 1)

                @block.vector
                def _(vector, t=t, bi=bi):
                    vector.wait_ge(p.h("qS"), qS_p1 + 16 * (t + 1))
                    vector.tensor_reduce(s1[:], xtile[bi][:], axis=AX.X,
                                         op=ALU.add).then_inc(p.bump("sV", 1), 1)
                    vector.wait_ge(p.h("sV"), p.val("sV"))
                    vector.tensor_scalar(mean[:], s1[:], 1.0 / D, None,
                                         op0=ALU.mult).then_inc(p.bump("sV", 1), 1)
                    vector.wait_ge(p.h("sV"), p.val("sV"))
                    vector.tensor_mul(m2[:], mean[:], mean[:]).then_inc(p.bump("sV", 1), 1)
                    vector.wait_ge(p.h("sS"), p.val("sS"))
                    vector.wait_ge(p.h("sV"), p.val("sV"))
                    vector.scalar_tensor_tensor(varv[:], s2[:], 1.0 / D, m2[:],
                                                op0=ALU.mult, op1=ALU.subtract).then_inc(p.bump("sV", 1), 1)
                    vector.wait_ge(p.h("sV"), p.val("sV"))
                    vector.tensor_scalar(varv[:], varv[:], EPS_LN, None,
                                         op0=ALU.add).then_inc(p.bump("sV", 1), 1)

                @block.scalar
                def _(scalar, t=t, bi=bi):
                    scalar.wait_ge(p.h("sV"), p.val("sV"))
                    scalar.activation(rstd[:], varv[:], AF.Ln).then_inc(p.bump("sS", 1), 1)
                    scalar.wait_ge(p.h("sS"), p.val("sS"))
                    scalar.activation(rstd[:], rstd[:], AF.Exp, scale=-0.5).then_inc(p.bump("sS", 1), 1)

                @block.vector
                def _(vector, t=t, bi=bi):
                    vector.wait_ge(p.h("sS"), p.val("sS"))
                    vector.tensor_scalar(bias2[:], mean[:], rstd[:, 0:1], -1.0,
                                         op0=ALU.mult, op1=ALU.mult).then_inc(p.bump("sV", 1), 1)

                @block.vector
                def _(vector, t=t, bi=bi):
                    vector.wait_ge(p.h("sV"), p.val("sV"))
                    vector.tensor_scalar(xntile[bi][:], xtile[bi][:], rstd[:, 0:1], bias2[:, 0:1],
                                         op0=ALU.mult, op1=ALU.add).then_inc(p.bump("sV", 1), 1)

                @block.gpsimd
                def _(gpsimd, t=t, bi=bi):
                    gpsimd.wait_ge(p.h("sV"), p.val("sV"))
                    gpsimd.dma_start(out=xn_dram[t * 128:(t + 1) * 128, :],
                                     in_=xntile[bi][:]).then_inc(p.bump("qG", 16), 16)

            t_xn = tap("xn", [S, D], BF16)
            if t_xn is not None:
                @block.sync
                def _(sync):
                    sync.wait_ge(p.h("qG"), p.val("qG"))
                    sync.dma_start(out=t_xn, in_=xn_dram).then_inc(p.bump("qS", 16), 16)

        qG_p1_done = p.val("qG")

        if stage < 2:
            _finish(p, block)
            return nc, tap_aps

        p2es = ExitStack()
        with p2es:
            xnT = p2es.enter_context(nc.sbuf_tensor("p2_xnT", [128, DT * S], BF16))
            wqsl = [p2es.enter_context(nc.sbuf_tensor(f"p2_wq{i}", [128, DT * 128], BF16)) for i in range(2)]
            ev = [p2es.enter_context(nc.sbuf_tensor(f"p2_ev{i}", [128, 512], BF16)) for i in range(4)]
            psum_qkv = [p2es.enter_context(nc.psum_tensor(f"p2_ps{i}", [128, 512], F32)) for i in range(2)]

            @block.sync
            def _(sync):
                sync.wait_ge(p.h("qG"), qG_p1_done)
                for dt in range(DT):
                    sync.dma_start_transpose(out=xnT[:, dt * S:(dt + 1) * S],
                                             in_=xn_dram[:, dt * 128:(dt + 1) * 128]).then_inc(p.bump("qS", 16), 16)

            qS_xnT = p.val("qS")

            ft_list = ([("q", ft, qT_dram, S) for ft in range(16)]
                       + [("k", ft, kT_dram, S) for ft in range(16)]
                       + [("v", ft, vT_dram, OWN) for ft in range(16)])
            cnts = [e[3] // 512 for e in ft_list]
            csum = np.cumsum(cnts).tolist()
            NG2 = csum[-1]

            sT_p2 = p.val("sT")
            qG_p2 = p.val("qG")
            sV_p2 = p.val("sV")
            qS_st2 = p.val("qS") + 16 * NG2 * 0

            @block.gpsimd
            def _(gpsimd):
                for i, (kind, ft, dst, ntok) in enumerate(ft_list):
                    col0 = {"q": 0, "k": D, "v": 2 * D}[kind] + ft * 128
                    if i == 0:
                        gpsimd.wait_ge(p.h("qG"), qG_p1_done)
                    if i >= 2:
                        gpsimd.wait_ge(p.h("sT"), sT_p2 + csum[i - 2])
                    src = wqkv[:, col0:col0 + 128].rearrange("(dt pp) c -> pp dt c", pp=128)
                    dstap = wqsl[i % 2][:].rearrange("pp (dt c) -> pp dt c", c=128)
                    gpsimd.dma_start(out=dstap, in_=src).then_inc(p.bump("qG", 16), 16)

            @block.tensor
            def _(tensor):
                k = 0
                tensor.wait_ge(p.h("qS"), qS_xnT)
                for i, (kind, ft, dst, ntok) in enumerate(ft_list):
                    tensor.wait_ge(p.h("qG"), qG_p2 + 16 * (i + 1))
                    for tc in range(ntok // 512):
                        if k >= 2:
                            tensor.wait_ge(p.h("sV"), sV_p2 + k - 1)
                        pb = psum_qkv[k % 2]
                        for dt in range(DT):
                            ins = tensor.matmul(pb[:],
                                          wqsl[i % 2][:, dt * 128:(dt + 1) * 128],
                                          xnT[:, dt * S + tc * 512: dt * S + tc * 512 + 512],
                                          start=(dt == 0), stop=(dt == DT - 1))
                        ins.then_inc(p.bump("sT", 1), 1)
                        k += 1

            qS_ev = p.val("qS")

            @block.vector
            def _(vector):
                k = 0
                for i, (kind, ft, dst, ntok) in enumerate(ft_list):
                    for tc in range(ntok // 512):
                        vector.wait_ge(p.h("sT"), sT_p2 + k + 1)
                        if k >= 4:
                            vector.wait_ge(p.h("qS"), qS_ev + 16 * (k - 3))
                        vector.tensor_copy(ev[k % 4][:], psum_qkv[k % 2][:]).then_inc(p.bump("sV", 1), 1)
                        k += 1

            @block.sync
            def _(sync):
                k = 0
                for i, (kind, ft, dst, ntok) in enumerate(ft_list):
                    for tc in range(ntok // 512):
                        sync.wait_ge(p.h("sV"), sV_p2 + k + 1)
                        sync.dma_start(out=dst[ft * 128:(ft + 1) * 128, tc * 512:(tc + 1) * 512],
                                       in_=ev[k % 4][:]).then_inc(p.bump("qS", 16), 16)
                        k += 1

            t_qT = tap("qT", [D, S], BF16)
            if t_qT is not None:
                @block.gpsimd
                def _(gpsimd):
                    gpsimd.wait_ge(p.h("qS"), p.val("qS"))
                    gpsimd.dma_start(out=t_qT, in_=qT_dram).then_inc(p.bump("qG", 16), 16)

        qS_qkv_done = p.val("qS")

        if stage < 3:
            _finish(p, block)
            return nc, tap_aps

        vt_scope = es.enter_context(ExitStack())
        vt_sb = vt_scope.enter_context(nc.sbuf_tensor("p3_vt", [128, NH * OWN], BF16))

        p3es = ExitStack()
        with p3es:
            kh = p3es.enter_context(nc.sbuf_tensor("p3_kh", [128, S], BF16))
            qh = p3es.enter_context(nc.sbuf_tensor("p3_qh", [128, S], BF16))
            Ebuf = [p3es.enter_context(nc.sbuf_tensor(f"p3_E{i}", [128, OWN], BF16)) for i in range(2)]
            Edump = p3es.enter_context(nc.sbuf_tensor("p3_Ed", [128, OWN], BF16))
            accT = p3es.enter_context(nc.sbuf_tensor("p3_acc", [128, OWN], F32))
            zcol = [p3es.enter_context(nc.sbuf_tensor(f"p3_z{i}", [128, 2], F32)) for i in range(2)]
            rz = [p3es.enter_context(nc.sbuf_tensor(f"p3_rz{i}", [128, 1], F32)) for i in range(2)]
            cs_sb = p3es.enter_context(nc.sbuf_tensor("p3_cs", [1, OWN], F32))

            t_cs = tap("colsum0", [1, OWN])

            for hd in range(NH):
                hes = ExitStack()
                with hes:
                    psum_s = [hes.enter_context(nc.psum_tensor(f"p3_s{hd}_{i}", [128, OWN], F32)) for i in range(2)]

                    sV_pre = p.val("sV")

                    @block.sync
                    def _(sync, hd=hd, sV_pre=sV_pre):
                        sync.wait_ge(p.h("qS"), qS_qkv_done)
                        if hd > 0:
                            sync.wait_ge(p.h("sV"), sV_pre)
                        sync.dma_start(out=kh[:], in_=kT_dram[hd * 128:(hd + 1) * 128, :]).then_inc(p.bump("qS", 16), 16)
                        sync.dma_start(out=qh[:], in_=qT_dram[hd * 128:(hd + 1) * 128, :]).then_inc(p.bump("qS", 16), 16)
                        sync.dma_start(out=vt_sb[:, hd * OWN:(hd + 1) * OWN],
                                       in_=vT_dram[hd * 128:(hd + 1) * 128, :]).then_inc(p.bump("qS", 16), 16)

                    qS_h = p.val("qS")
                    sT_h0 = p.val("sT")
                    sS_h0 = p.val("sS")
                    sV_h0 = p.val("sV")

                    @block.tensor
                    def _(tensor, qS_h=qS_h, sS_h0=sS_h0):
                        tensor.wait_ge(p.h("qS"), qS_h)
                        for qt in range(NQT):
                            for half in range(2):
                                hidx = 2 * qt + half
                                if hidx >= 2:
                                    tensor.wait_ge(p.h("sS"), sS_h0 + 5 * ((hidx - 2) // 2) + ((hidx - 2) % 2) + 1)
                                for kc in range(4):
                                    ins = tensor.matmul(psum_s[half][:, kc * 512:(kc + 1) * 512],
                                                        qh[:, qt * 128:(qt + 1) * 128],
                                                        kh[:, half * OWN + kc * 512: half * OWN + kc * 512 + 512],
                                                        start=True, stop=True)
                                ins.then_inc(p.bump("sT", 1), 1)

                    @block.scalar
                    def _(scalar, sT_h0=sT_h0, sV_h0=sV_h0):
                        for qt in range(NQT):
                            scalar.wait_ge(p.h("sT"), sT_h0 + 2 * qt + 1)
                            if qt >= 2:
                                scalar.wait_ge(p.h("sV"), sV_h0 + qt)
                            scalar.activation(Ebuf[qt % 2][:], psum_s[0][:], AF.Exp,
                                              scale=INVSQ,
                                              accum_out=zcol[qt % 2][:, 0:1]).then_inc(p.bump("sS", 1), 1)
                            scalar.wait_ge(p.h("sT"), sT_h0 + 2 * qt + 2)
                            scalar.activation(Edump[:], psum_s[1][:], AF.Exp,
                                              scale=INVSQ,
                                              accum_out=zcol[qt % 2][:, 1:2]).then_inc(p.bump("sS", 1), 1)
                            scalar.wait_ge(p.h("sS"), p.val("sS"))
                            scalar.add(rz[qt % 2][:], zcol[qt % 2][:, 0:1], zcol[qt % 2][:, 1:2]).then_inc(p.bump("sS", 1), 1)
                            scalar.wait_ge(p.h("sS"), p.val("sS"))
                            scalar.activation(rz[qt % 2][:], rz[qt % 2][:], AF.Ln).then_inc(p.bump("sS", 1), 1)
                            scalar.wait_ge(p.h("sS"), p.val("sS"))
                            scalar.activation(rz[qt % 2][:], rz[qt % 2][:], AF.Exp,
                                              scale=-1.0).then_inc(p.bump("sS", 1), 1)

                    @block.vector
                    def _(vector, sS_h0=sS_h0):
                        vector.memset(accT[:], 0.0).then_inc(p.bump("sV", 1), 1)
                        vector.wait_ge(p.h("sV"), p.val("sV"))
                        for qt in range(NQT):
                            vector.wait_ge(p.h("sS"), sS_h0 + 5 * (qt + 1))
                            vector.scalar_tensor_tensor(accT[:], Ebuf[qt % 2][:],
                                                        rz[qt % 2][:, 0:1], accT[:],
                                                        op0=ALU.mult, op1=ALU.add).then_inc(p.bump("sV", 1), 1)

                    sV_acc_done = p.val("sV")

                hes2 = ExitStack()
                with hes2:
                    psum_bc = hes2.enter_context(nc.psum_tensor(f"p3_bc{hd}", [128, OWN], F32))

                    @block.tensor
                    def _(tensor, sV_acc_done=sV_acc_done):
                        tensor.wait_ge(p.h("sV"), sV_acc_done)
                        for c in range(OWN // 512):
                            ins = tensor.matmul(psum_bc[:, c * 512:(c + 1) * 512], ones_sq[:],
                                          accT[:, c * 512:(c + 1) * 512], start=True, stop=True)
                        ins.then_inc(p.bump("sT", 1), 1)

                    if t_cs is not None and hd == 0:
                        @block.scalar
                        def _(scalar):
                            scalar.wait_ge(p.h("sT"), p.val("sT"))
                            scalar.copy(cs_sb[:], psum_bc[0:1, :]).then_inc(p.bump("sS", 1), 1)

                        @block.sync
                        def _(sync):
                            sync.wait_ge(p.h("sS"), p.val("sS"))
                            sync.dma_start(out=t_cs, in_=cs_sb[:]).then_inc(p.bump("qS", 16), 16)

                    @block.vector
                    def _(vector, hd=hd, qS_h=qS_h):
                        vector.wait_ge(p.h("sT"), p.val("sT"))
                        vector.wait_ge(p.h("qS"), qS_h)
                        vector.tensor_tensor(vt_sb[:, hd * OWN:(hd + 1) * OWN],
                                             vt_sb[:, hd * OWN:(hd + 1) * OWN],
                                             psum_bc[:], op=ALU.mult).then_inc(p.bump("sV", 1), 1)

        sV_p3_done = p.val("sV")

        if stage < 4:
            _finish(p, block)
            return nc, tap_aps

        p4es = ExitStack()
        with p4es:
            wo_sb = p4es.enter_context(nc.sbuf_tensor("p4_wo", [128, NH * D], BF16))
            psum_o = [p4es.enter_context(nc.psum_tensor(f"p4_ps{i}", [128, 512], F32)) for i in range(2)]
            oev = [p4es.enter_context(nc.sbuf_tensor(f"p4_ev{i}", [128, 512], BF16)) for i in range(4)]

            qG_wo0 = p.val("qG")

            @block.gpsimd
            def _(gpsimd):
                gpsimd.wait_ge(p.h("sV"), sV_p3_done)
                for hd in range(NH):
                    gpsimd.dma_start(out=wo_sb[:, hd * D:(hd + 1) * D],
                                     in_=wo[hd * 128:(hd + 1) * 128, :]).then_inc(p.bump("qG", 16), 16)

            qG_wo = p.val("qG")
            sT_p4 = p.val("sT")
            sV_p4 = p.val("sV")
            NG4 = DT * (OWN // 512)

            @block.tensor
            def _(tensor):
                k = 0
                tensor.wait_ge(p.h("qG"), qG_wo)
                tensor.wait_ge(p.h("sV"), sV_p3_done)
                for dm in range(DT):
                    for tc in range(OWN // 512):
                        if k >= 2:
                            tensor.wait_ge(p.h("sV"), sV_p4 + k - 1)
                        pb = psum_o[k % 2]
                        for hd in range(NH):
                            ins = tensor.matmul(pb[:],
                                          wo_sb[:, hd * D + dm * 128: hd * D + dm * 128 + 128],
                                          vt_sb[:, hd * OWN + tc * 512: hd * OWN + tc * 512 + 512],
                                          start=(hd == 0), stop=(hd == NH - 1))
                        ins.then_inc(p.bump("sT", 1), 1)
                        k += 1

            qS_p4 = p.val("qS")

            @block.vector
            def _(vector):
                k = 0
                for dm in range(DT):
                    for tc in range(OWN // 512):
                        vector.wait_ge(p.h("sT"), sT_p4 + k + 1)
                        if k >= 4:
                            vector.wait_ge(p.h("qS"), qS_p4 + 16 * (k - 3))
                        vector.tensor_copy(oev[k % 4][:], psum_o[k % 2][:]).then_inc(p.bump("sV", 1), 1)
                        k += 1

            @block.sync
            def _(sync):
                k = 0
                for dm in range(DT):
                    for tc in range(OWN // 512):
                        sync.wait_ge(p.h("sV"), sV_p4 + k + 1)
                        sync.dma_start(out=attnT_dram[dm * 128:(dm + 1) * 128, tc * 512:(tc + 1) * 512],
                                       in_=oev[k % 4][:]).then_inc(p.bump("qS", 16), 16)
                        k += 1

            t_attnT = tap("attnT", [D, OWN], BF16)
            if t_attnT is not None:
                @block.gpsimd
                def _(gpsimd):
                    gpsimd.wait_ge(p.h("qS"), p.val("qS"))
                    gpsimd.dma_start(out=t_attnT, in_=attnT_dram).then_inc(p.bump("qG", 16), 16)

        vt_scope.close()
        qS_attn_done = p.val("qS")

        if stage < 5:
            _finish(p, block)
            return nc, tap_aps

        TOKC = 512
        NBLK = 256
        HGR = 2
        NGRP = (HT + HGR - 1) // HGR
        p5es = ExitStack()
        with p5es:
            hid = p5es.enter_context(nc.sbuf_tensor("p5_hid", [128, HT * TOKC], BF16))
            xnc = p5es.enter_context(nc.sbuf_tensor("p5_xnc", [128, DT * TOKC], BF16))
            w1g = [p5es.enter_context(nc.sbuf_tensor(f"p5_w1g{i}", [128, DT * 128 * HGR], BF16)) for i in range(2)]
            w2g = [p5es.enter_context(nc.sbuf_tensor(f"p5_w2g{i}", [128, DT * 128 * HGR], BF16)) for i in range(2)]
            w3sg = [p5es.enter_context(nc.sbuf_tensor(f"p5_w3sg{i}", [128, 6 * D], BF16)) for i in range(2)]
            out_acc = [p5es.enter_context(nc.sbuf_tensor(f"p5_oa{i}", [128, D], F32)) for i in range(2)]
            gate = [p5es.enter_context(nc.sbuf_tensor(f"p5_gate{i}", [128, 512], F32)) for i in range(2)]
            xmrg = [p5es.enter_context(nc.sbuf_tensor(f"p5_xm{i}", [128, D], F32)) for i in range(2)]
            amrg = [p5es.enter_context(nc.sbuf_tensor(f"p5_am{i}", [128, D], BF16)) for i in range(2)]
            omrg = [p5es.enter_context(nc.sbuf_tensor(f"p5_om{i}", [128, D], F32)) for i in range(2)]

            for ck in range(OWN // TOKC):
                tok0 = ck * TOKC

                sT_prev_all = p.val("sT")
                sV_prev_mrg = p.val("sV")

                @block.sync
                def _(sync, ck=ck, tok0=tok0, sT_prev_all=sT_prev_all):
                    sync.wait_ge(p.h("qS"), qS_attn_done)
                    if ck > 0:
                        sync.wait_ge(p.h("sT"), sT_prev_all)
                    for dt in range(DT):
                        sync.dma_start_transpose(out=xnc[:, dt * TOKC:(dt + 1) * TOKC],
                                                 in_=xn_dram[tok0:tok0 + TOKC, dt * 128:(dt + 1) * 128]).then_inc(p.bump("qS", 16), 16)

                qS_xnc = p.val("qS")

                NHALF = TOKC // 512
                grp_nht = [min(HGR, HT - g * HGR) for g in range(NGRP)]
                grp_mm = [2 * n * NHALF for n in grp_nht]
                grp_mm_csum = np.cumsum(grp_mm).tolist()

                sT_pr = p.val("sT")
                qG_w12 = p.val("qG")
                sS_pr = p.val("sS")
                sV_pr = p.val("sV")

                @block.gpsimd
                def _(gpsimd, ck=ck, sT_pr=sT_pr):
                    if ck == 0:
                        gpsimd.wait_ge(p.h("qS"), qS_attn_done)
                    for g in range(NGRP):
                        nht = grp_nht[g]
                        if g < 2:
                            gpsimd.wait_ge(p.h("sT"), sT_pr)
                        else:
                            gpsimd.wait_ge(p.h("sT"), sT_pr + grp_mm_csum[g - 2])
                        for wsl, wdr in ((w1g, w1qT_dram), (w2g, w2qT_dram)):
                            for dt in range(DT):
                                gpsimd.dma_start(
                                    out=wsl[g % 2][:, dt * 128 * HGR: dt * 128 * HGR + nht * 128],
                                    in_=wdr[dt * 128:(dt + 1) * 128,
                                            g * HGR * 128: g * HGR * 128 + nht * 128]).then_inc(p.bump("qG", 16), 16)

                pres = ExitStack()
                with pres:
                    psx1 = [pres.enter_context(nc.psum_tensor(f"p5_x1_{ck}_{i}", [128, 512], F32)) for i in range(2)]
                    psx2 = [pres.enter_context(nc.psum_tensor(f"p5_x2_{ck}_{i}", [128, 512], F32)) for i in range(2)]

                    @block.tensor
                    def _(tensor, ck=ck, qS_xnc=qS_xnc, qG_w12=qG_w12, sV_pr=sV_pr, sV_prev_mrg=sV_prev_mrg):
                        npr = 0
                        tensor.wait_ge(p.h("qS"), qS_xnc)
                        if ck > 0:
                            tensor.wait_ge(p.h("sV"), sV_prev_mrg)
                        for g in range(NGRP):
                            nht = grp_nht[g]
                            tensor.wait_ge(p.h("qG"), qG_w12 + 16 * 2 * DT * (g + 1))
                            for hi in range(nht):
                                for half in range(NHALF):
                                    if npr >= 2:
                                        tensor.wait_ge(p.h("sV"), sV_pr + npr - 1)
                                    for dt in range(DT):
                                        ins = tensor.matmul(psx1[npr % 2][:],
                                                      w1g[g % 2][:, dt * 128 * HGR + hi * 128: dt * 128 * HGR + hi * 128 + 128],
                                                      xnc[:, dt * TOKC + half * 512: dt * TOKC + half * 512 + 512],
                                                      start=(dt == 0), stop=(dt == DT - 1))
                                    ins.then_inc(p.bump("sT", 1), 1)
                                    for dt in range(DT):
                                        ins = tensor.matmul(psx2[npr % 2][:],
                                                      w2g[g % 2][:, dt * 128 * HGR + hi * 128: dt * 128 * HGR + hi * 128 + 128],
                                                      xnc[:, dt * TOKC + half * 512: dt * TOKC + half * 512 + 512],
                                                      start=(dt == 0), stop=(dt == DT - 1))
                                    ins.then_inc(p.bump("sT", 1), 1)
                                    npr += 1

                    @block.scalar
                    def _(scalar, ck=ck, sT_pr=sT_pr):
                        k = 0
                        for g in range(NGRP):
                            for hi in range(grp_nht[g]):
                                for half in range(NHALF):
                                    scalar.wait_ge(p.h("sT"), sT_pr + 2 * k + 1)
                                    scalar.activation(gate[k % 2][:], psx1[k % 2][:],
                                                      AF.Sigmoid).then_inc(p.bump("sS", 1), 1)
                                    k += 1

                    @block.vector
                    def _(vector, ck=ck, sT_pr=sT_pr, sS_pr=sS_pr, sT_prev_all=sT_prev_all):
                        k = 0
                        vector.wait_ge(p.h("sT"), sT_prev_all)
                        for g in range(NGRP):
                            for hi in range(grp_nht[g]):
                                ht = g * HGR + hi
                                for half in range(NHALF):
                                    vector.wait_ge(p.h("sS"), sS_pr + k + 1)
                                    vector.wait_ge(p.h("sT"), sT_pr + 2 * k + 2)
                                    vector.tensor_tensor(hid[:, ht * TOKC + half * 512: ht * TOKC + half * 512 + 512],
                                                         gate[k % 2][:], psx2[k % 2][:],
                                                         op=ALU.mult).then_inc(p.bump("sV", 1), 1)
                                    k += 1

                sV_hid_done = p.val("sV")

                if ck == 0:
                    t_hid = tap("hid0", [128, HT * TOKC], BF16)
                    if t_hid is not None:
                        @block.gpsimd
                        def _(gpsimd):
                            gpsimd.wait_ge(p.h("sV"), sV_hid_done)
                            gpsimd.dma_start(out=t_hid, in_=hid[:]).then_inc(p.bump("qG", 16), 16)

                SGH = 6
                sg_list = [(s * SGH, min(SGH, HT - s * SGH)) for s in range((HT + SGH - 1) // SGH)]
                NSG = len(sg_list)

                for blk in range(TOKC // NBLK):
                    btok = tok0 + blk * NBLK
                    sV_mrg_prev = p.val("sV")
                    ces = ExitStack()
                    with ces:
                        pso = [ces.enter_context(nc.psum_tensor(f"p5_o_{ck}_{blk}_{i}", [128, D], F32)) for i in range(2)]

                        qG_w3 = p.val("qG")
                        sT_c = p.val("sT")
                        sV_ev0 = p.val("sV")

                        @block.gpsimd
                        def _(gpsimd, ck=ck, blk=blk, sT_c=sT_c):
                            for s, (ht0, nht) in enumerate(sg_list):
                                if s < 2:
                                    gpsimd.wait_ge(p.h("sT"), sT_c)
                                else:
                                    gpsimd.wait_ge(p.h("sT"), sT_c + 2 * (s - 1))
                                gpsimd.dma_start(
                                    out=w3sg[s % 2][:, 0:nht * D].rearrange("pp (t d) -> pp t d", d=D),
                                    in_=w3qT_dram[ht0 * 128:(ht0 + nht) * 128, :].rearrange("(t pp) d -> pp t d", pp=128)
                                ).then_inc(p.bump("qG", 16), 16)

                        @block.tensor
                        def _(tensor, ck=ck, blk=blk, sV_hid_done=sV_hid_done, qG_w3=qG_w3, sV_ev0=sV_ev0, sV_mrg_prev=sV_mrg_prev):
                            tensor.wait_ge(p.h("sV"), sV_hid_done)
                            if not (ck == 0 and blk == 0):
                                tensor.wait_ge(p.h("sV"), sV_mrg_prev)
                            for s, (ht0, nht) in enumerate(sg_list):
                                tensor.wait_ge(p.h("qG"), qG_w3 + 16 * (s + 1))
                                if s >= 1:
                                    tensor.wait_ge(p.h("sV"), sV_ev0 + 2 * s)
                                for tt in range(NBLK // 128):
                                    for hi in range(nht):
                                        ht = ht0 + hi
                                        for dmc in range(D // 512):
                                            ins = tensor.matmul(
                                                pso[tt][:, dmc * 512:(dmc + 1) * 512],
                                                hid[:, ht * TOKC + blk * NBLK + tt * 128: ht * TOKC + blk * NBLK + tt * 128 + 128],
                                                w3sg[s % 2][:, hi * D + dmc * 512: hi * D + dmc * 512 + 512],
                                                start=(hi == 0), stop=(hi == nht - 1))
                                    ins.then_inc(p.bump("sT", 1), 1)

                        @block.vector
                        def _(vector, ck=ck, blk=blk, sT_c=sT_c, sV_ev0=sV_ev0):
                            for s in range(NSG):
                                for tt in range(NBLK // 128):
                                    vector.wait_ge(p.h("sT"), sT_c + 2 * s + tt + 1)
                                    if s == 0:
                                        vector.tensor_copy(out_acc[tt][:], pso[tt][:]).then_inc(p.bump("sV", 1), 1)
                                    else:
                                        vector.tensor_tensor(out_acc[tt][:], out_acc[tt][:], pso[tt][:],
                                                             op=ALU.add).then_inc(p.bump("sV", 1), 1)

                        sV_acc_out = p.val("sV")

                        if ck == 0 and blk == 0:
                            t_oa = tap("ffn00", [128, D])
                            if t_oa is not None:
                                @block.gpsimd
                                def _(gpsimd, sV_acc_out=sV_acc_out):
                                    gpsimd.wait_ge(p.h("sV"), sV_acc_out)
                                    gpsimd.dma_start(out=t_oa, in_=out_acc[0][:]).then_inc(p.bump("qG", 16), 16)

                        @block.sync
                        def _(sync, ck=ck, blk=blk, btok=btok, sV_mrg_prev=sV_mrg_prev):
                            if not (ck == 0 and blk == 0):
                                sync.wait_ge(p.h("sV"), sV_mrg_prev)
                            for tt in range(NBLK // 128):
                                sync.dma_start(out=xmrg[tt][:],
                                               in_=xb[btok + tt * 128: btok + tt * 128 + 128, :]).then_inc(p.bump("qS", 16), 16)
                                sync.dma_start_transpose(out=amrg[tt][:],
                                                         in_=attnT_dram[:, btok + tt * 128: btok + tt * 128 + 128]).then_inc(p.bump("qS", 16), 16)

                        qS_mrg = p.val("qS")

                        @block.vector
                        def _(vector, ck=ck, blk=blk, qS_mrg=qS_mrg, sV_acc_out=sV_acc_out):
                            vector.wait_ge(p.h("sV"), sV_acc_out)
                            vector.wait_ge(p.h("qS"), qS_mrg)
                            for tt in range(NBLK // 128):
                                vector.tensor_add(omrg[tt][:], out_acc[tt][:], xmrg[tt][:]).then_inc(p.bump("sV", 1), 1)
                                vector.wait_ge(p.h("sV"), p.val("sV"))
                                vector.tensor_add(omrg[tt][:], omrg[tt][:], amrg[tt][:]).then_inc(p.bump("sV", 1), 1)

                        sV_mrg = p.val("sV")

                        @block.sync
                        def _(sync, ck=ck, blk=blk, btok=btok, sV_mrg=sV_mrg):
                            sync.wait_ge(p.h("sV"), sV_mrg)
                            for tt in range(NBLK // 128):
                                sync.dma_start(out=out[btok + tt * 128: btok + tt * 128 + 128, :],
                                               in_=omrg[tt][:]).then_inc(p.bump("qS", 16), 16)

        @block.sync
        def _(sync):
            sync.wait_ge(p.h("qS"), p.val("qS"))
            sync.wait_ge(p.h("qG"), p.val("qG"))

    return nc, tap_aps


def kernel(L0_attn_x, L0_attn_w_qkv, L0_attn_w_o, L0_ffn_w1, L0_ffn_w2, L0_ffn_w3,
           _taps=(), _stage=99.0):
    x = np.ascontiguousarray(np.asarray(L0_attn_x, dtype=np.float32))
    wqkv = np.ascontiguousarray(np.asarray(L0_attn_w_qkv, dtype=np.float32))
    wo = np.ascontiguousarray(np.asarray(L0_attn_w_o, dtype=np.float32))
    w1 = np.ascontiguousarray(np.asarray(L0_ffn_w1, dtype=np.float32))
    w2 = np.ascontiguousarray(np.asarray(L0_ffn_w2, dtype=np.float32))
    w3 = np.ascontiguousarray(np.asarray(L0_ffn_w3, dtype=np.float32))

    nc, tap_aps = build_program(taps=_taps, stage=_stage)

    in_maps = []
    for c in range(8):
        b, g = divmod(c, 2)
        own = x[b, g * OWN:(g + 1) * OWN, :]
        other = x[b, (1 - g) * OWN:(2 - g) * OWN, :]
        xb_c = np.ascontiguousarray(np.concatenate([own, other], axis=0))
        in_maps.append({"xb": xb_c, "wqkv": wqkv, "wo": wo,
                        "w1": w1, "w2": w2, "w3": w3})

    res = run_bass_kernel_spmd(nc, in_maps, core_ids=list(range(8)))

    outv = np.empty((B, S, D), dtype=np.float32)
    for c in range(8):
        b, g = divmod(c, 2)
        outv[b, g * OWN:(g + 1) * OWN, :] = res.results[c]["out"]
    if _taps:
        kernel._tap_results = [{k: r["tap_" + k] for k in _taps} for r in res.results]
    return outv
